# version 06
# baseline (speedup 1.0000x reference)
import contextlib
import numpy as np
import ml_dtypes
import concourse.bass as bass
import concourse.mybir as mybir
from concourse.bass_utils import run_bass_kernel_spmd

F32 = mybir.dt.float32
BF16 = mybir.dt.bfloat16
I32 = mybir.dt.int32
FP8 = mybir.dt.float8e4
AF = mybir.ActivationFunctionType
OP = mybir.AluOpType
AX = mybir.AxisListType.X

B, D = 4, 1024
LOG127 = float(np.log(127.0))
LOG_HALF = float(np.log(0.5))
DI, NST, RK, DCONV = 2048, 16, 64, 4
E2 = 2 * DI
INNER = 4096
EPS = 1e-5
W = 256
TOK = W // 128
G = DI // 128
NR = 3


class Prog:
    ENGS = ("sync", "scalar", "vector", "tensor", "gpsimd")

    def __init__(self, nc, es):
        self.nc, self.es = nc, es
        self.ops = {e: [] for e in self.ENGS}
        self.sems, self.cnt = {}, {}

    def sem(self, name):
        if name not in self.sems:
            self.sems[name] = self.es.enter_context(self.nc.semaphore(name))
            self.cnt[name] = 0
        return self.sems[name]

    def emit(self, eng, fn, inc=None, inc_by=1):
        if inc is not None:
            s = self.sem(inc)
            self.cnt[inc] += inc_by
            tgt = self.cnt[inc]
            def wrapped(e, fn=fn, s=s, n=inc_by):
                fn(e).then_inc(s, n)
            self.ops[eng].append(wrapped)
            return (inc, tgt)
        self.ops[eng].append(fn)
        return None

    def wait(self, eng, evt):
        if evt is None:
            return
        name, tgt = evt
        s = self.sems[name]
        self.ops[eng].append(lambda e, s=s, t=tgt: e.wait_ge(s, t))

    def now(self, name):
        self.sem(name)
        return (name, self.cnt[name])

    def build(self, block):
        def mk(oplist):
            def body(e):
                for op in oplist:
                    op(e)
            return body
        for eng in self.ENGS:
            getattr(block, eng)(mk(self.ops[eng]))


def build_program(L, debug=False):
    NW = L // W
    dbg_ck = {}
    nc = bass.Bass("TRN2", target_bir_lowering=False,
                   detect_race_conditions=False, debug=False)

    def din(name, shape, dt=F32):
        return nc.dram_tensor(name, shape, dt, kind="ExternalInput").ap()

    x_in = din("x", [L, D])
    lnw_in = din("ln_w", [1, D]); lnb_in = din("ln_b", [1, D])
    w_inT = din("w_inT", [D, E2])
    convw_in = din("conv_w", [DI, DCONV]); convb_in = din("conv_b", [DI, 1])
    wxT_in = din("w_xprojT", [DI, 96])
    wdtT_in = din("w_dtT", [RK, DI])
    bdt_in = din("b_dt", [DI, 1])
    alog_in = din("a_log", [DI, NST])
    dskip_in = din("d_skip", [DI, 1])
    woT_in = din("w_outT", [DI, D])
    wgT_in = din("w_gateT", [D, INNER])
    wuT_in = din("w_upT", [D, INNER])
    wdnT_in = din("w_downT", [INNER, D])
    consts_in = din("consts", [1, 8])
    ident_in = din("ident", [128, 128], BF16)
    ones_in = din("ones128", [128, 1])

    out_hbm = nc.dram_tensor("out", [L, D], F32, kind="ExternalOutput").ap()

    wi_img = nc.dram_tensor("wi_img", [D, E2], BF16, kind="Internal").ap()
    wo_img = nc.dram_tensor("wo_img", [DI, D], BF16, kind="Internal").ap()
    wg_img = nc.dram_tensor("wg_img", [D, INNER], FP8, kind="Internal").ap()
    wu_img = nc.dram_tensor("wu_img", [D, INNER], FP8, kind="Internal").ap()
    wdn_img = nc.dram_tensor("wdn_img", [INNER, D], FP8, kind="Internal").ap()
    bc_scr = nc.dram_tensor("bc_scr", [32, W], F32, kind="Internal").ap()
    sc_scr = nc.dram_tensor("sc_scr", [1, 4], F32, kind="Internal").ap()

    es = contextlib.ExitStack()
    p = Prog(nc, es)

    def sb(name, shape, dt=F32):
        return es.enter_context(nc.sbuf_tensor(name, shape, dt))

    with es:
        convw = sb("convw", [128, DCONV * G])
        convb = sb("convb", [128, G])
        negbdt = sb("negbdt", [128, G])
        negA = sb("negA", [128, NST * G])
        dskip = sb("dskip", [128, G])
        lnw_bc = sb("lnw_bc", [128, D]); lnb_bc = sb("lnb_bc", [128, D])
        ident = sb("sident", [128, 128], BF16)
        ones128 = sb("sones", [128, 1])
        wxT = sb("wxT", [128, 96 * G])
        wdtT = sb("wdtT", [RK, DI])
        mwv = sb("mwv", [128, 8])
        cvec = sb("cvec", [128, 8])

        wiA = sb("wiA", [128, 8 * 128], BF16); wiB = sb("wiB", [128, 8 * 128], BF16)
        woA = sb("woA", [128, D], BF16); woB = sb("woB", [128, D], BF16)
        wfA = sb("wfA", [128, 8 * 512], FP8)
        wfC = sb("wfC", [128, 8 * 512], FP8)
        wdnS = sb("wdnS", [128, 16 * 512], FP8)

        xw = [sb(f"xw{t}", [128, D]) for t in range(TOK)]
        xn_b = sb("xn_b", [128, D], BF16)
        xnT = sb("xnT", [128, 8 * W], BF16)
        u_f = sb("u_f", [128, W * G])
        lnsb = [sb(f"lns{i}", [128, W]) for i in range(2)]
        sz = sb("sz", [128, W * G])
        yac = [sb(f"yac{i}", [128, W]) for i in range(2)]
        ygb = sb("ygb", [128, W * G], BF16)
        xdbl = sb("xdbl", [96, W])
        Ball = sb("Ball", [128, W * NST])
        Call = sb("Call", [128, W * NST])
        hcar = sb("hcar", [128, NST * G])
        xst = sb("xst", [128, 4 * G])
        xs_buf = sb("xs_buf", [128, 4 + W])
        xc_buf = sb("xc_buf", [128, W])
        wg_buf = sb("wg_buf", [128, W])
        dAr = [sb(f"dA{i}", [128, W]) for i in range(NR)]
        dBr = [sb(f"dB{i}", [128, W]) for i in range(NR)]
        hr = [sb(f"h{i}", [128, W]) for i in range(NR)]
        hCb = [sb(f"hC{i}", [128, W]) for i in range(2)]

        s1 = sb("s1", [128, 1]); s2 = sb("s2", [128, 1])
        mu1 = sb("mu1", [128, 1]); mu1n = sb("mu1n", [128, 1])
        rr1 = sb("rr1", [128, 1]); rr2 = sb("rr2", [128, 1])
        tA = sb("tA", [128, 1]); tB = sb("tB", [128, 1]); tC = sb("tC", [128, 1])
        tR = sb("tR", [128, 1])
        ln_scr = sb("ln_scr", [128, D])

        out1 = sb("out1", [128, TOK * D])
        xn1b = sb("xn1b", [128, TOK * D])
        xq_b = sb("xq_b", [128, D], BF16)
        xqT = sb("xqT", [128, D], BF16)
        pool = sb("pool", [128, 1536])
        q_i = sb("q_i", [128, 512], I32)
        xq2_b = sb("xq2_b", [128, 512], BF16)
        xq2T = sb("xq2T", [128, INNER], BF16)
        amax = sb("amax", [128, 1]); sqv = sb("sqv", [128, 1])
        scg = sb("scg", [128, 1]); scu = sb("scu", [128, 1]); scd = sb("scd", [128, 1])
        out2 = sb("out2", [128, D])
        wq8 = sb("wq8", [128, INNER], FP8)
        wacc = sb("wacc", [128, 32])
        wsum = sb("wsum", [128, 1])

        psA = es.enter_context(nc.psum_tensor("psA", [128, 1536], F32))
        psB = es.enter_context(nc.psum_tensor("psB", [128, 1024], F32))
        ptA = es.enter_context(nc.psum_tensor("ptA", [128, 1024], BF16))
        ptB = es.enter_context(nc.psum_tensor("ptB", [128, 1024], BF16))

        blk = es.enter_context(nc.Block())

        SD, GD, ST = "sdma", "gdma", "store"
        for s_ in (SD, GD, ST, "vector", "scalar", "tensor", "gpsimd"):
            p.sem(s_)

        def v(fn, inc=False):
            return p.emit("vector", fn, inc="vector" if inc else None)

        def a(fn, inc=False):
            return p.emit("scalar", fn, inc="scalar" if inc else None)

        def t(fn, inc=False):
            return p.emit("tensor", fn, inc="tensor" if inc else None)

        def g_(fn, inc=False):
            return p.emit("gpsimd", fn, inc="gpsimd" if inc else None)

        def dma(out, in_, **kw):
            return p.emit("sync", lambda e: e.dma_start(out=out, in_=in_, **kw),
                          inc=SD, inc_by=16)

        def gdma(out, in_, **kw):
            return p.emit("gpsimd", lambda e: e.dma_start(out=out, in_=in_, **kw),
                          inc=GD, inc_by=16)

        def stdma(out, in_, **kw):
            return p.emit("sync", lambda e: e.dma_start(out=out, in_=in_, **kw),
                          inc=ST, inc_by=16)

        def barrier():
            evs = {}
            evs["vector"] = v(lambda e: e.engine_nop(), inc=True)
            evs["gpsimd"] = g_(lambda e: e.engine_nop(), inc=True)
            evs["scalar"] = a(lambda e: e.activation(out=tR[0:1, 0:1], in_=ones128[0:1, 0:1], func=AF.Identity), inc=True)
            evs["tensor"] = t(lambda e: e.matmul(out=psA[0:1, 1284:1285],
                                                 lhsT=ones128[0:1, :],
                                                 rhs=ones128[0:1, :],
                                                 start=True, stop=True), inc=True)
            evs["sync"] = dma(mwv[0:1, 7:8], ones_in[0:1, 0:1])
            for eng in Prog.ENGS:
                for other, evt in evs.items():
                    if other != eng:
                        p.wait(eng, evt)

        dma(cvec[:], consts_in.broadcast_to([128, 8]))
        dma(ident[:], ident_in)
        dma(ones128[:], ones_in)
        dma(lnw_bc[:], lnw_in.broadcast_to([128, D]))
        dma(lnb_bc[:], lnb_in.broadcast_to([128, D]))
        for g in range(G):
            dma(convw[:, g * DCONV:(g + 1) * DCONV], convw_in[g * 128:(g + 1) * 128, :])
            dma(convb[:, g:g + 1], convb_in[g * 128:(g + 1) * 128, :])
            dma(negbdt[:, g:g + 1], bdt_in[g * 128:(g + 1) * 128, :])
            dma(negA[:, g * NST:(g + 1) * NST], alog_in[g * 128:(g + 1) * 128, :])
            dma(dskip[:, g:g + 1], dskip_in[g * 128:(g + 1) * 128, :])
        ev = p.now(SD)
        p.wait("scalar", ev)
        a(lambda e: e.activation(out=negA[:], in_=negA[:], func=AF.Exp))
        a(lambda e: e.activation(out=negbdt[:], in_=negbdt[:], func=AF.Identity,
                                 scale=-1.0))

        for g in range(G):
            dma(wxT[:, g * 96:(g + 1) * 96], wxT_in[g * 128:(g + 1) * 128, :])
        dma(wdtT[:], wdtT_in)
        for i in range(D // 128):
            gdma(wi_img[i * 128:(i + 1) * 128, :], w_inT[i * 128:(i + 1) * 128, :])
        for i in range(G):
            gdma(wo_img[i * 128:(i + 1) * 128, :], woT_in[i * 128:(i + 1) * 128, :])

        def quant_weight(src, dst, n_rows, col):
            nt = n_rows // 128
            cols = src.shape[1]
            ev_rd = None
            for i in range(nt):
                p.wait("sync", ev_rd)
                evl = dma(u_f[:, 0:cols], src[i * 128:(i + 1) * 128, :])
                p.wait("scalar", evl)
                ev_rd = a(lambda e, c=cols, i=i: e.activation(
                    out=sz[:, 0:c], in_=u_f[:, 0:c],
                    func=AF.Abs, accum_out=wacc[:, i:i + 1]), inc=True)
            p.wait("vector", ev_rd)
            evv = v(lambda e, nt=nt: e.reduce_sum(out=wsum[:], in_=wacc[:, 0:nt],
                                                  axis=AX), inc=True)
            p.wait("tensor", evv)
            et = t(lambda e: e.matmul(out=psA[0:1, 1280:1281], lhsT=wsum[:],
                                      rhs=ones128[:], start=True, stop=True), inc=True)
            p.wait("scalar", et)
            nel = float(n_rows * cols)
            a(lambda e, nel=nel: e.activation(out=tA[0:1, 0:1], in_=psA[0:1, 1280:1281],
                                     func=AF.Relu, scale=1.0 / nel, bias=cvec[0:1, 1:2]))
            a(lambda e: e.activation(out=tR[0:1, 0:1], in_=tA[0:1, 0:1],
                                     func=AF.Ln, bias=cvec[0:1, 0:1]))
            a(lambda e: e.activation(out=ln_scr[0:1, 0:1], in_=tR[0:1, 0:1],
                                     func=AF.Exp, bias=cvec[0:1, 3:4]))
            a(lambda e: e.activation(out=ln_scr[0:1, 1:2], in_=tR[0:1, 0:1],
                                     func=AF.Exp, bias=cvec[0:1, 4:5]))
            ev1 = a(lambda e: e.activation(out=ln_scr[0:1, 2:3], in_=ln_scr[0:1, 1:2],
                                           func=AF.Identity, scale=-1.0), inc=True)
            p.wait("sync", ev1)
            ev2 = dma(sc_scr[0:1, 0:3], ln_scr[0:1, 0:3])
            p.wait("sync", ev2)
            dma(mwv[:, col:col + 1], sc_scr[0:1, 0:1].broadcast_to([128, 1]))
            dma(mwv[:, col + 3:col + 4], sc_scr[0:1, 1:2].broadcast_to([128, 1]))
            ev3 = dma(mwv[:, 6:7], sc_scr[0:1, 2:3].broadcast_to([128, 1]))
            p.wait("vector", ev3)
            ev_st = None
            for i in range(nt):
                p.wait("sync", ev_rd)
                evl = dma(u_f[:, 0:cols], src[i * 128:(i + 1) * 128, :])
                p.wait("vector", evl)
                p.wait("vector", ev_st)
                v(lambda e, c=cols: e.tensor_scalar(
                    out=sz[:, 0:c], in0=u_f[:, 0:c],
                    scalar1=mwv[:, 6:7], scalar2=None, op0=OP.is_le))
                ev_rd = v(lambda e, c=cols, col=col: e.scalar_tensor_tensor(
                    out=wq8[:, 0:c], in0=u_f[:, 0:c], scalar=mwv[:, col + 3:col + 4],
                    in1=sz[:, 0:c], op0=OP.is_ge, op1=OP.subtract),
                    inc=True)
                p.wait("sync", ev_rd)
                ev_st = dma(dst[i * 128:(i + 1) * 128, :], wq8[:, 0:cols])

        quant_weight(wgT_in, wg_img, D, 0)
        quant_weight(wuT_in, wu_img, D, 1)
        quant_weight(wdnT_in, wdn_img, INNER, 2)

        v(lambda e: e.memset(xst[:], 0.0))
        v(lambda e: e.memset(mwv[:, 7:8], 0.0))
        v(lambda e: e.memset(hcar[:], 0.0))

        barrier()

        def emit_double_ln(src_tile, guard_v, xn1_slot=None):
            p.wait("scalar", guard_v)
            a(lambda e: e.activation(out=ln_scr[:], in_=src_tile[:],
                                     func=AF.Identity, accum_out=s1[:]))
            a(lambda e: e.activation(out=ln_scr[:], in_=src_tile[:],
                                     func=AF.Square, accum_out=s2[:]))
            a(lambda e: e.activation(out=mu1[:], in_=s1[:], func=AF.Identity,
                                     scale=1.0 / D))
            a(lambda e: e.activation(out=mu1n[:], in_=s1[:], func=AF.Identity,
                                     scale=-1.0 / D))
            a(lambda e: e.activation(out=tB[:], in_=s2[:], func=AF.Identity,
                                     scale=1.0 / D, bias=cvec[:, 0:1]))
            ea = a(lambda e: e.activation(out=tA[:], in_=mu1[:], func=AF.Square),
                   inc=True)
            p.wait("vector", ea)
            ev1 = v(lambda e: e.tensor_sub(tC[:], tB[:], tA[:]), inc=True)
            p.wait("scalar", ev1)
            a(lambda e: e.activation(out=tR[:], in_=tC[:], func=AF.Ln))
            ea = a(lambda e: e.activation(out=rr1[:], in_=tR[:], func=AF.Exp,
                                          scale=-0.5), inc=True)
            p.wait("vector", ea)
            ev1 = v(lambda e: e.tensor_mul(tC[:], mu1n[:], rr1[:]), inc=True)
            p.wait("scalar", ev1)
            a(lambda e: e.activation(out=ln_scr[:], in_=src_tile[:],
                                     func=AF.Square, scale=rr1[:],
                                     bias=tC[:], accum_out=s2[:]))
            a(lambda e: e.activation(out=tB[:], in_=s2[:], func=AF.Identity,
                                     scale=1.0 / D, bias=cvec[:, 0:1]))
            a(lambda e: e.activation(out=tR[:], in_=tB[:], func=AF.Ln))
            ea = a(lambda e: e.activation(out=rr2[:], in_=tR[:], func=AF.Exp,
                                          scale=-0.5), inc=True)
            p.wait("vector", ea)
            if xn1_slot is not None:
                v(lambda e: e.tensor_scalar(out=xn1_slot, in0=src_tile[:],
                                            scalar1=mu1[:], scalar2=rr1[:],
                                            op0=OP.subtract, op1=OP.mult))
            v(lambda e: e.tensor_scalar(out=ln_scr[:], in0=src_tile[:],
                                        scalar1=mu1[:], scalar2=rr1[:],
                                        op0=OP.subtract, op1=OP.mult))
            v(lambda e: e.scalar_tensor_tensor(out=ln_scr[:], in0=ln_scr[:],
                                               scalar=rr2[:], in1=lnw_bc[:],
                                               op0=OP.mult, op1=OP.mult))
            return v(lambda e: e.tensor_add(xn_b[:], ln_scr[:], lnb_bc[:]),
                     inc=True)

        def emit_single_ln(src_ap, guard_v):
            p.wait("scalar", guard_v)
            a(lambda e: e.activation(out=ln_scr[:], in_=src_ap,
                                     func=AF.Identity, accum_out=s1[:]))
            a(lambda e: e.activation(out=ln_scr[:], in_=src_ap,
                                     func=AF.Square, accum_out=s2[:]))
            a(lambda e: e.activation(out=mu1[:], in_=s1[:], func=AF.Identity,
                                     scale=1.0 / D))
            a(lambda e: e.activation(out=tB[:], in_=s2[:], func=AF.Identity,
                                     scale=1.0 / D, bias=cvec[:, 0:1]))
            ea = a(lambda e: e.activation(out=tA[:], in_=mu1[:], func=AF.Square),
                   inc=True)
            p.wait("vector", ea)
            ev1 = v(lambda e: e.tensor_sub(tC[:], tB[:], tA[:]), inc=True)
            p.wait("scalar", ev1)
            a(lambda e: e.activation(out=tR[:], in_=tC[:], func=AF.Ln))
            ea = a(lambda e: e.activation(out=rr1[:], in_=tR[:], func=AF.Exp,
                                          scale=-0.5), inc=True)
            p.wait("vector", ea)
            return v(lambda e: e.tensor_scalar(out=ln_scr[:], in0=src_ap,
                                               scalar1=mu1[:], scalar2=rr1[:],
                                               op0=OP.subtract, op1=OP.mult), inc=True)

        for w in range(NW):
            ev_x = []
            for tt in range(TOK):
                ev_x.append(dma(xw[tt][:], x_in[w * W + tt * 128:w * W + (tt + 1) * 128, :]))

            guard = None
            for tt in range(TOK):
                p.wait("scalar", ev_x[tt])
                ev_cast = emit_double_ln(xw[tt], guard, xn1b[:, tt * D:(tt + 1) * D])
                guard = ev_cast
                p.wait("tensor", ev_cast)
                for k in range(8):
                    pt = ptA if k % 2 == 0 else ptB
                    et = t(lambda e, k=k, pt=pt: e.transpose(
                        out=pt[:, 0:128], in_=xn_b[:, k * 128:(k + 1) * 128],
                        identity=ident[:]), inc=True)
                    p.wait("vector", et)
                    ev_tp = v(lambda e, k=k, tt=tt, pt=pt: e.tensor_copy(
                        xnT[:, k * W + tt * 128:k * W + tt * 128 + 128],
                        pt[:, 0:128]), inc=True)
                    p.wait("tensor", ev_tp)
            ev_xnT = p.now("vector")
            p.wait("tensor", ev_xnT)

            ev_wi = [None, None]
            ev_ps_free = None
            ev_u = [None] * G
            ev_sz = [None] * G
            for of in range(E2 // 128):
                buf = wiA if of % 2 == 0 else wiB
                p.wait("sync", ev_wi[of % 2])
                evl = dma(buf[:].rearrange("p (k c) -> p k c", k=8),
                          wi_img[:, of * 128:(of + 1) * 128]
                          .rearrange("(k p) c -> p k c", p=128))
                p.wait("tensor", evl)
                p.wait("tensor", ev_ps_free)
                for k in range(7):
                    t(lambda e, k=k, buf=buf: e.matmul(
                        out=psA[:, 0:W], lhsT=buf[:, k * 128:(k + 1) * 128],
                        rhs=xnT[:, k * W:(k + 1) * W], start=(k == 0), stop=False))
                et = t(lambda e, buf=buf: e.matmul(
                    out=psA[:, 0:W], lhsT=buf[:, 7 * 128:8 * 128],
                    rhs=xnT[:, 7 * W:8 * W], start=False, stop=True), inc=True)
                ev_wi[of % 2] = et
                g = of % 16
                if of < 16:
                    p.wait("vector", et)
                    if of >= 1:
                        p.wait("vector", ev_u[g - 1])
                    v(lambda e, g=g: e.tensor_copy(xs_buf[:, 0:3],
                                                   xst[:, g * 4:g * 4 + 3]))
                    ev_e = v(lambda e: e.tensor_copy(xs_buf[:, 3:3 + W], psA[:, 0:W]),
                             inc=True)
                    ev_ps_free = ev_e
                    v(lambda e, g=g: e.tensor_scalar(
                        out=xc_buf[:], in0=xs_buf[:, 0:W],
                        scalar1=convw[:, g * 4:g * 4 + 1], scalar2=convb[:, g:g + 1],
                        op0=OP.mult, op1=OP.add))
                    for kk in range(1, 3):
                        v(lambda e, g=g, kk=kk: e.scalar_tensor_tensor(
                            out=xc_buf[:], in0=xs_buf[:, kk:kk + W],
                            scalar=convw[:, g * 4 + kk:g * 4 + kk + 1],
                            in1=xc_buf[:], op0=OP.mult, op1=OP.add))
                    ev_c = v(lambda e, g=g: e.scalar_tensor_tensor(
                        out=xc_buf[:], in0=xs_buf[:, 3:3 + W],
                        scalar=convw[:, g * 4 + 3:g * 4 + 4],
                        in1=xc_buf[:], op0=OP.mult, op1=OP.add), inc=True)
                    v(lambda e, g=g: e.tensor_copy(xst[:, g * 4:g * 4 + 3],
                                                   xs_buf[:, W:W + 3]))
                    p.wait("scalar", ev_c)
                    ev_u[g] = a(lambda e, g=g: e.activation(
                        out=u_f[:, g * W:(g + 1) * W], in_=xc_buf[:], func=AF.Silu),
                        inc=True)
                else:
                    p.wait("scalar", et)
                    ea = a(lambda e, g=g: e.activation(
                        out=sz[:, g * W:(g + 1) * W], in_=psA[:, 0:W], func=AF.Silu),
                        inc=True)
                    ev_sz[g] = ea
                    ev_ps_free = ea

            ev_ub = None
            for g in range(G):
                p.wait("tensor", ev_u[g])
                ev_ub = t(lambda e, g=g: e.matmul(
                    out=psA[0:96, 1024:1024 + W], lhsT=wxT[:, g * 96:(g + 1) * 96],
                    rhs=u_f[:, g * W:(g + 1) * W],
                    start=(g == 0), stop=(g == G - 1)), inc=(g == G - 1))
            p.wait("vector", ev_ub)
            ev_xd = v(lambda e: e.tensor_copy(xdbl[:], psA[0:96, 1024:1024 + W]),
                      inc=True)

            ev_dtr = ev_xd
            p.wait("sync", ev_xd)
            ev_bs = dma(bc_scr[:], xdbl[64:96, :])
            p.wait("sync", ev_bs)
            ev_B = dma(Ball[:], bc_scr[0:16, :].rearrange("n w -> (n w)")[None, :]
                       .broadcast_to([128, NST * W]))
            ev_C = dma(Call[:], bc_scr[16:32, :].rearrange("n w -> (n w)")[None, :]
                       .broadcast_to([128, NST * W]))
            if debug and w == NW - 1:
                ckx = nc.dram_tensor("ck_xdbl_a", [96, W], F32, kind="ExternalOutput").ap()
                ckb = nc.dram_tensor("ck_Ball_a", [128, NST * W], F32, kind="ExternalOutput").ap()
                p.wait("sync", ev_C)
                stdma(ckx, xdbl[:])
                stdma(ckb, Ball[:])

            dA_used = [None] * NR
            dB_used = [None] * NR
            h_used = [None] * NR
            ev_dt_ps = None
            ev_lns_used = [None, None]
            for g in range(G):
                lb = lnsb[g % 2]
                p.wait("tensor", ev_dtr)
                p.wait("tensor", ev_dt_ps)
                et = t(lambda e, g=g: e.matmul(
                    out=psA[:, 512:512 + W], lhsT=wdtT[:, g * 128:(g + 1) * 128],
                    rhs=xdbl[0:64, :], start=True, stop=True), inc=True)
                p.wait("scalar", et)
                p.wait("scalar", ev_lns_used[g % 2])
                ev_sg = a(lambda e, g=g, lb=lb: e.activation(
                    out=lb[:], in_=psA[:, 512:512 + W], func=AF.Sigmoid,
                    scale=-1.0, bias=negbdt[:, g:g + 1]), inc=True)
                ev_dt_ps = ev_sg
                ev_lns = a(lambda e, lb=lb: e.activation(out=lb[:], in_=lb[:],
                                                         func=AF.Ln), inc=True)
                p.wait("vector", ev_lns)
                p.wait("vector", ev_u[g])
                ev_wg = v(lambda e, g=g, lb=lb: e.scalar_tensor_tensor(
                    out=wg_buf[:], in0=lb[:], scalar=-1.0,
                    in1=u_f[:, g * W:(g + 1) * W], op0=OP.mult, op1=OP.mult), inc=True)
                p.wait("gpsimd", ev_wg)
                p.wait("gpsimd", ev_B)

                for n in range(NST):
                    k = n % NR
                    p.wait("scalar", dA_used[k])
                    ev_dA = a(lambda e, g=g, n=n, k=k, lb=lb: e.activation(
                        out=dAr[k][:], in_=lb[:], func=AF.Exp,
                        scale=negA[:, g * NST + n:g * NST + n + 1]), inc=True)
                    if n == NST - 1:
                        ev_lns_used[g % 2] = ev_dA
                    p.wait("gpsimd", dB_used[k])
                    ev_dB = g_(lambda e, n=n, k=k: e.tensor_tensor(
                        out=dBr[k][:], in0=wg_buf[:], in1=Ball[:, n * W:(n + 1) * W],
                        op=OP.mult), inc=True)
                    p.wait("vector", h_used[k])
                    p.wait("vector", ev_dA)
                    p.wait("vector", ev_dB)
                    ev_h = v(lambda e, g=g, n=n, k=k: e.tensor_tensor_scan(
                        out=hr[k][:], data0=dAr[k][:], data1=dBr[k][:],
                        initial=hcar[:, g * NST + n:g * NST + n + 1],
                        op0=OP.mult, op1=OP.add), inc=True)
                    dA_used[k] = ev_h
                    dB_used[k] = ev_h
                    p.wait("gpsimd", ev_h)
                    ev_cc = g_(lambda e, g=g, n=n, k=k: e.tensor_copy(
                        hcar[:, g * NST + n:g * NST + n + 1], hr[k][:, W - 1:W]),
                        inc=True)
                    h_used[k] = ev_cc
                    p.wait("vector", ev_C)
                    if n == 0:
                        v(lambda e, g=g, k=k, n=n: e.tensor_mul(
                            yac[g % 2][:], hr[k][:], Call[:, n * W:(n + 1) * W]))
                    else:
                        v(lambda e, k=k, n=n: e.tensor_mul(
                            hCb[n % 2][:], hr[k][:], Call[:, n * W:(n + 1) * W]))
                        v(lambda e, g=g, n=n: e.tensor_add(
                            yac[g % 2][:], yac[g % 2][:], hCb[n % 2][:]))

                v(lambda e, g=g: e.scalar_tensor_tensor(
                    out=yac[g % 2][:], in0=u_f[:, g * W:(g + 1) * W],
                    scalar=dskip[:, g:g + 1], in1=yac[g % 2][:],
                    op0=OP.mult, op1=OP.add))
                p.wait("vector", ev_sz[g])
                v(lambda e, g=g: e.tensor_mul(
                    ygb[:, g * W:(g + 1) * W], yac[g % 2][:],
                    sz[:, g * W:(g + 1) * W]), inc=True)

            ev_ygb = p.now("vector")
            if debug and w == NW - 1:
                ckx2 = nc.dram_tensor("ck_xdbl_b", [96, W], F32, kind="ExternalOutput").ap()
                ckb2 = nc.dram_tensor("ck_Ball_b", [128, NST * W], F32, kind="ExternalOutput").ap()
                p.wait("sync", ev_ygb)
                stdma(ckx2, xdbl[:])
                stdma(ckb2, Ball[:])

            def wops(tt, dh):
                return (psA[:, 0:512], psA[:, 512:1024],
                        psB[:, 0:512], psB[:, 512:1024])[tt * 2 + dh]

            p.wait("tensor", ev_ygb)
            ev_wo = [None, None]
            ev_wo_last = None
            for g in range(G):
                buf = woA if g % 2 == 0 else woB
                p.wait("sync", ev_wo[g % 2])
                evl = dma(buf[:], wo_img[g * 128:(g + 1) * 128, :])
                p.wait("tensor", evl)
                for tt in range(TOK):
                    for dh in range(2):
                        last = (tt == TOK - 1 and dh == 1)
                        ev_mm = t(lambda e, g=g, tt=tt, dh=dh, buf=buf: e.matmul(
                            out=wops(tt, dh),
                            lhsT=ygb[:, g * W + tt * 128:g * W + tt * 128 + 128],
                            rhs=buf[:, dh * 512:(dh + 1) * 512],
                            start=(g == 0), stop=(g == G - 1)),
                            inc=last)
                        if last:
                            ev_wo[g % 2] = ev_mm
                            ev_wo_last = ev_mm
            p.wait("vector", ev_wo_last)
            ev_out1 = None
            for tt in range(TOK):
                for dh in range(2):
                    sl_ = slice(tt * D + dh * 512, tt * D + (dh + 1) * 512)
                    v(lambda e, tt=tt, dh=dh, sl_=sl_: e.tensor_add(
                        out1[:, sl_], wops(tt, dh),
                        xw[tt][:, dh * 512:(dh + 1) * 512]))
                    ev_out1 = v(lambda e, sl_=sl_: e.tensor_add(
                        out1[:, sl_], out1[:, sl_], xn1b[:, sl_]),
                        inc=(tt == TOK - 1 and dh == 1))

            ev_store_prev = None
            guard = ev_out1
            for tt in range(TOK):
                o1 = lambda sl, tt=tt: out1[:, tt * D + sl.start:tt * D + sl.stop]
                ev_xnf = emit_single_ln(out1[:, tt * D:(tt + 1) * D], guard)
                ev_am = v(lambda e: e.reduce_max(out=amax[:], in_=ln_scr[:], axis=AX,
                                                 apply_absolute_value=True), inc=True)
                p.wait("scalar", ev_am)
                a(lambda e: e.activation(out=tA[:], in_=amax[:], func=AF.Relu,
                                         bias=cvec[:, 1:2]))
                a(lambda e: e.activation(out=tB[:], in_=tA[:], func=AF.Identity,
                                         bias=cvec[:, 0:1]))
                a(lambda e: e.activation(out=tR[:], in_=tA[:], func=AF.Ln,
                                         bias=cvec[:, 0:1]))
                ev_sq = a(lambda e: e.activation(out=sqv[:], in_=tR[:], func=AF.Exp,
                                                 scale=-1.0, bias=cvec[:, 2:3]), inc=True)
                p.wait("vector", ev_sq)
                v(lambda e: e.tensor_mul(scg[:], tB[:], mwv[:, 0:1]))
                ev_sc = v(lambda e: e.tensor_mul(scu[:], tB[:], mwv[:, 1:2]),
                          inc=True)
                p.wait("scalar", ev_sc)
                ev_qa = None
                for sl in range(2):
                    ea2 = a(lambda e, sl=sl: e.activation(
                        out=pool[:, 0:512], in_=ln_scr[:, sl * 512:(sl + 1) * 512],
                        func=AF.Identity, scale=sqv[:]), inc=True)
                    p.wait("vector", ea2)
                    ev_qi = v(lambda e: e.tensor_copy(q_i[:], pool[:, 0:512]),
                              inc=True)
                    p.wait("scalar", ev_qi)
                    ev_qa = a(lambda e, sl=sl: e.activation(
                        out=xq_b[:, sl * 512:(sl + 1) * 512], in_=q_i[:],
                        func=AF.Identity), inc=True)
                p.wait("tensor", ev_qa)
                for k in range(8):
                    pt = ptA if k % 2 == 0 else ptB
                    et = t(lambda e, k=k, pt=pt: e.transpose(
                        out=pt[:, 0:128], in_=xq_b[:, k * 128:(k + 1) * 128],
                        identity=ident[:]), inc=True)
                    p.wait("vector", et)
                    ev_tp = v(lambda e, k=k, pt=pt: e.tensor_copy(
                        xqT[:, k * 128:(k + 1) * 128], pt[:, 0:128]), inc=True)
                    p.wait("tensor", ev_tp)
                ev_xqT = p.now("vector")
                p.wait("tensor", ev_xqT)

                ev_wfg = None
                ev_wfu = None
                ev_innerj = None
                ev_gs = None
                for j in range(8):
                    bg, bu = wfA, wfC
                    p.wait("sync", ev_wfg)
                    evg = dma(bg[:].rearrange("p (k c) -> p k c", k=8),
                              wg_img[:, j * 512:(j + 1) * 512]
                              .rearrange("(k p) c -> p k c", p=128))
                    p.wait("sync", ev_wfu)
                    evu = dma(bu[:].rearrange("p (k c) -> p k c", k=8),
                              wu_img[:, j * 512:(j + 1) * 512]
                              .rearrange("(k p) c -> p k c", p=128))
                    p.wait("tensor", evg)
                    p.wait("tensor", evu)
                    p.wait("tensor", ev_innerj)
                    p.wait("tensor", ev_gs)
                    for k in range(7):
                        t(lambda e, k=k, bg=bg: e.matmul(
                            out=psB[:, 0:512], lhsT=xqT[:, k * 128:(k + 1) * 128],
                            rhs=bg[:, k * 512:(k + 1) * 512],
                            start=(k == 0), stop=False))
                    etg = t(lambda e, bg=bg: e.matmul(
                        out=psB[:, 0:512], lhsT=xqT[:, 7 * 128:8 * 128],
                        rhs=bg[:, 7 * 512:8 * 512], start=False, stop=True), inc=True)
                    for k in range(7):
                        t(lambda e, k=k, bu=bu: e.matmul(
                            out=psB[:, 512:1024], lhsT=xqT[:, k * 128:(k + 1) * 128],
                            rhs=bu[:, k * 512:(k + 1) * 512],
                            start=(k == 0), stop=False))
                    etu = t(lambda e, bu=bu: e.matmul(
                        out=psB[:, 512:1024], lhsT=xqT[:, 7 * 128:8 * 128],
                        rhs=bu[:, 7 * 512:8 * 512], start=False, stop=True), inc=True)
                    ev_wfg = etg
                    ev_wfu = etu
                    p.wait("scalar", etg)
                    ev_gs = a(lambda e: e.activation(out=ln_scr[:, 0:512],
                                                     in_=psB[:, 0:512], func=AF.Silu,
                                                     scale=scg[:]), inc=True)
                    p.wait("vector", ev_gs)
                    p.wait("vector", etu)
                    ev_innerj = v(lambda e, j=j: e.scalar_tensor_tensor(
                        out=u_f[:, j * 512:(j + 1) * 512],
                        in0=psB[:, 512:1024], scalar=scu[:], in1=ln_scr[:, 0:512],
                        op0=OP.mult, op1=OP.mult), inc=True)

                p.wait("vector", ev_innerj)
                ev_am = v(lambda e: e.reduce_max(out=amax[:], in_=u_f[:, 0:INNER],
                                                 axis=AX, apply_absolute_value=True),
                          inc=True)
                p.wait("scalar", ev_am)
                a(lambda e: e.activation(out=tA[:], in_=amax[:], func=AF.Relu,
                                         bias=cvec[:, 1:2]))
                a(lambda e: e.activation(out=tB[:], in_=tA[:], func=AF.Identity,
                                         bias=cvec[:, 0:1]))
                a(lambda e: e.activation(out=tR[:], in_=tA[:], func=AF.Ln,
                                         bias=cvec[:, 0:1]))
                ev_sq = a(lambda e: e.activation(out=sqv[:], in_=tR[:], func=AF.Exp,
                                                 scale=-1.0, bias=cvec[:, 2:3]), inc=True)
                p.wait("vector", ev_sq)
                ev_sc2 = v(lambda e: e.tensor_mul(scd[:], tB[:], mwv[:, 2:3]),
                           inc=True)
                p.wait("scalar", ev_sc2)
                ev_q2 = None
                for j in range(8):
                    p.wait("scalar", ev_q2)
                    ea2 = a(lambda e, j=j: e.activation(
                        out=pool[:, 0:512], in_=u_f[:, j * 512:(j + 1) * 512],
                        func=AF.Identity, scale=sqv[:]), inc=True)
                    p.wait("vector", ea2)
                    ev_qi = v(lambda e: e.tensor_copy(q_i[:], pool[:, 0:512]),
                              inc=True)
                    p.wait("scalar", ev_qi)
                    ev_q2a = a(lambda e: e.activation(out=xq2_b[:], in_=q_i[:],
                                                      func=AF.Identity), inc=True)
                    p.wait("tensor", ev_q2a)
                    ev_tp = None
                    for k4 in range(4):
                        kk = j * 4 + k4
                        pt = ptA if kk % 2 == 0 else ptB
                        et = t(lambda e, k4=k4, pt=pt: e.transpose(
                            out=pt[:, 0:128], in_=xq2_b[:, k4 * 128:(k4 + 1) * 128],
                            identity=ident[:]), inc=True)
                        p.wait("vector", et)
                        ev_tp = v(lambda e, kk=kk, pt=pt: e.tensor_copy(
                            xq2T[:, kk * 128:(kk + 1) * 128], pt[:, 0:128]), inc=True)
                        p.wait("tensor", ev_tp)
                    ev_q2 = p.now("tensor")
                ev_xq2T = p.now("vector")
                p.wait("tensor", ev_xq2T)

                ev_dn = None
                for dh in range(2):
                    etd = None
                    for half in range(2):
                        p.wait("sync", ev_dn)
                        evd = dma(wdnS[:].rearrange("p (k c) -> p k c", k=16),
                                  wdn_img[half * 2048:(half + 1) * 2048,
                                          dh * 512:(dh + 1) * 512]
                                  .rearrange("(k p) c -> p k c", p=128))
                        p.wait("tensor", evd)
                        p.wait("tensor", ev_store_prev)
                        for k in range(16):
                            kk = half * 16 + k
                            lastk = (half == 1 and k == 15)
                            etd = t(lambda e, k=k, kk=kk, half=half: e.matmul(
                                out=psB[:, 0:512],
                                lhsT=xq2T[:, kk * 128:(kk + 1) * 128],
                                rhs=wdnS[:, k * 512:(k + 1) * 512],
                                start=(kk == 0), stop=(kk == 31)), inc=lastk or k == 15)
                        ev_dn = etd
                    p.wait("vector", etd)
                    p.wait("vector", ev_store_prev)
                    ev_o2 = v(lambda e, dh=dh, tt=tt: e.scalar_tensor_tensor(
                        out=out2[:, dh * 512:(dh + 1) * 512], in0=psB[:, 0:512],
                        scalar=scd[:],
                        in1=out1[:, tt * D + dh * 512:tt * D + (dh + 1) * 512],
                        op0=OP.mult, op1=OP.add), inc=True)
                    p.wait("sync", ev_o2)
                ev_store_prev = stdma(
                    out_hbm[w * W + tt * 128:w * W + (tt + 1) * 128, :], out2[:])
                guard = ev_o2

            barrier()

        if debug:
            dbg_names = ["xnT", "u_f", "sz", "ygb", "hcar", "out1", "Ball", "Call",
                         "xdbl", "lns0", "lns1", "mwv", "negA", "negbdt", "wg_buf",
                         "xst", "dA0", "dB0", "h0", "out2", "ln_scr", "xq_b", "xqT",
                         "xq2T", "pool", "xnb"]
            dbg_map = dict(xnT=xnT, u_f=u_f, sz=sz, ygb=ygb, hcar=hcar, out1=out1,
                           Ball=Ball, Call=Call, xdbl=xdbl, lns0=lnsb[0],
                           lns1=lnsb[1], mwv=mwv, negA=negA, negbdt=negbdt,
                           wg_buf=wg_buf, xst=xst, dA0=dAr[0], dB0=dBr[0], h0=hr[0],
                           out2=out2, ln_scr=ln_scr, xq_b=xq_b, xqT=xqT, xq2T=xq2T,
                           pool=pool, xnb=xn_b)
            for nm in dbg_names:
                tile = dbg_map[nm]
                sh = list(tile.shape)
                dt_ = tile.dtype
                dout = nc.dram_tensor(f"dbg_{nm}", sh, dt_, kind="ExternalOutput").ap()
                stdma(dout, tile[:])
        total_store = p.cnt[ST]
        p.ops["sync"].append(
            lambda e, s=p.sems[ST], t_=total_store: e.wait_ge(s, t_))

        p.build(blk)
    return nc


_BUILD_CACHE = {}


def _get_program(L):
    if L not in _BUILD_CACHE:
        _BUILD_CACHE[L] = build_program(L)
    return _BUILD_CACHE[L]


def host_inputs(inputs):
    f32c = lambda x: np.ascontiguousarray(np.asarray(x, dtype=np.float32))
    return {
        "ln_w": f32c(inputs["L0_attn_ln_w"]).reshape(1, D),
        "ln_b": f32c(inputs["L0_attn_ln_b"]).reshape(1, D),
        "w_inT": f32c(np.asarray(inputs["L0_attn_w_in"]).T),
        "conv_w": f32c(inputs["L0_attn_conv_w"]),
        "conv_b": f32c(inputs["L0_attn_conv_b"]).reshape(DI, 1),
        "w_xprojT": f32c(np.asarray(inputs["L0_attn_w_xproj"]).T),
        "w_dtT": f32c(np.asarray(inputs["L0_attn_w_dt"]).T),
        "b_dt": f32c(inputs["L0_attn_b_dt"]).reshape(DI, 1),
        "a_log": f32c(inputs["L0_attn_a_log"]),
        "d_skip": f32c(inputs["L0_attn_d_skip"]).reshape(DI, 1),
        "w_outT": f32c(np.asarray(inputs["L0_attn_w_out"]).T),
        "w_gateT": f32c(np.asarray(inputs["L0_ffn_w_gate"]).T),
        "w_upT": f32c(np.asarray(inputs["L0_ffn_w_up"]).T),
        "w_downT": f32c(np.asarray(inputs["L0_ffn_w_down"]).T),
        "consts": np.array([[1e-5, -1e-5, LOG127, -LOG127, LOG_HALF, 0.0, 0.0, 0.0]],
                            dtype=np.float32),
        "ident": np.eye(128, dtype=ml_dtypes.bfloat16),
        "ones128": np.ones((128, 1), dtype=np.float32),
    }


def kernel(L0_attn_x, **inputs):
    inputs = dict(inputs, L0_attn_x=L0_attn_x)
    x = np.asarray(L0_attn_x, dtype=np.float32)
    Bq, Lq, Dq = x.shape
    nc = _get_program(Lq)
    com = host_inputs(inputs)
    in_maps = []
    for c in range(8):
        b = c // 2 if Bq == 4 else c % Bq
        m = dict(com)
        m["x"] = np.ascontiguousarray(x[b])
        in_maps.append(m)
    res = run_bass_kernel_spmd(nc, in_maps, core_ids=list(range(8)))
    out = np.empty((Bq, Lq, Dq), dtype=np.float32)
    for b in range(Bq):
        c = 2 * b if Bq == 4 else b
        out[b] = res.results[c]["out"]
    return out
